# version 11 (final)
# speedup vs baseline: 1.1928x; 1.0237x over previous
import numpy as np
import ml_dtypes
import concourse.bass as bass
import concourse.mybir as mybir
from concourse.bass_utils import run_bass_kernel_spmd

F32 = mybir.dt.float32
BF16 = mybir.dt.bfloat16
U32 = mybir.dt.uint32
AF = mybir.ActivationFunctionType
OP = mybir.AluOpType
AX = mybir.AxisListType

NCORES = 8
NB = 16
SEQ = 2048
D = 1024
H = 16
HD = 64
HL = 2
POS = SEQ // NCORES
NT = NB * POS
GT = NCORES * NT
TC = NT // 128
KR = 70
NTP = 32
ATT_SERIAL = False

BF = ml_dtypes.bfloat16



class Sem:

    def __init__(self, stack, nc, name):
        self.h = stack.enter_context(nc.semaphore(name))
        self.total = 0


class SemView:

    __slots__ = ("p", "base", "n")

    def __init__(self, phys):
        self.p = phys
        self.base = phys.total
        self.n = 0

    def inc(self, instr, k=1):
        instr.then_inc(self.p.h, k)
        self.n += k
        self.p.total += k
        return self.n

    def wait(self, eng, v):
        eng.wait_ge(self.p.h, self.base + v)


CONST_VALUES = [0.0, 1.0, -1.0, 1e-5, -0.5, 0.5, 0.125, 1.0 / D]


class Ctx:
    def __init__(self, nc, stack):
        self.nc = nc
        self.stack = stack
        self.sems = {}
        self.pools = [[], [], []]
        self.pi = 2
        self.cursor = 0
        self.t = {}
        self.cvals = {v: i for i, v in enumerate(CONST_VALUES)}

    def phase(self):
        self.pi = (self.pi + 1) % 3
        self.cursor = 0

    def sem(self, name, g=False):
        if name not in self.sems:
            if g:
                self.sems[name] = SemView(Sem(self.stack, self.nc, name))
            else:
                pool = self.pools[self.pi]
                i = self.cursor
                self.cursor += 1
                while i >= len(pool):
                    pool.append(Sem(self.stack, self.nc,
                                    "sp%d_%d" % (self.pi, len(pool))))
                self.sems[name] = SemView(pool[i])
        return self.sems[name]

    def cc(self, val, parts=128):
        i = self.cvals[val]
        return self.t["constpool"].ap()[0:parts, i:i + 1]

    def emit_consts(self, block):
        self.phase()
        cs = self.sem("consts")

        @block.vector
        def _(v):
            ins = None
            for val, i in self.cvals.items():
                ins = v.memset(self.t["constpool"].ap()[:, i:i + 1], float(val))
            cs.inc(ins)
        return cs

    def sbuf(self, name, shape, dtype, scope=None):
        h = (scope or self.stack).enter_context(self.nc.sbuf_tensor(name, shape, dtype))
        self.t[name] = h
        return h

    def psum(self, name, shape, dtype=F32):
        h = self.stack.enter_context(self.nc.psum_tensor(name, shape, dtype))
        self.t[name] = h
        return h

    def dram(self, name, shape, dtype, kind=None):
        if kind:
            h = self.nc.dram_tensor(name, shape, dtype, kind=kind)
        else:
            h = self.nc.dram_tensor(name, shape, dtype)
        self.t[name] = h
        return h



def p_layernorm(ctx, block, tag, src_name, wait_sems, gate_x=None, gate_t=None):
    ctx.phase()
    xsrc = ctx.t[src_name]
    uimg = ctx.t["u_img"]
    xb = ctx.t["xbuf"]
    st = ctx.t["ln_stats"]
    usg = ctx.t["ln_ustage"]
    sq = ctx.t["ln_sqscratch"]
    s_xl = [ctx.sem(tag + "_xl0"), ctx.sem(tag + "_xl1")]
    s_red = ctx.sem(tag + "_red")
    s_sq = ctx.sem(tag + "_sq")
    s_var = ctx.sem(tag + "_var")
    s_rstd = ctx.sem(tag + "_rstd")
    s_nb = ctx.sem(tag + "_nb")
    s_u = ctx.sem(tag + "_u")
    s_w = [ctx.sem(tag + "_w0"), ctx.sem(tag + "_w1")]
    s_tt = ctx.sem(tag + "_tt")
    s_tw = ctx.sem(tag + "_tw")
    uTimg = ctx.t["uT_img"]
    uTs = ctx.t["ln_uts"]

    def stat(i, tc):
        return st.ap()[:, 4 * tc + i:4 * tc + i + 1]

    def xsl(tc):
        return xb.ap()[:, (tc % 2) * D:(tc % 2 + 1) * D]

    def _emit_trans(s, g):
        if gate_t:
            for sem, val in gate_t(g):
                sem.wait(s, val)
        s_w[0].wait(s, 16 * (2 * g + 2))
        s_w[1].wait(s, 16 * (2 * g + 2))
        if g >= 1:
            s_tw.wait(s, 16 * g)
        for c in range(8):
            s_tt.inc(s.dma_start(
                out=uTs.ap()[:, c * 512:(c + 1) * 512],
                in_=uimg.ap()[g * 512:(g + 1) * 512, c * 128:(c + 1) * 128],
                transpose=True), 16)
        s_tt.wait(s, 16 * 8 * (g + 1))
        s_tw.inc(s.dma_start(
            out=uTimg.ap()[g * D:(g + 1) * D, :]
            .rearrange("(c p) n -> p c n", p=128),
            in_=uTs.ap().rearrange("p (c n) -> p c n", n=512)), 16)

    @block.sync
    def _(s):
        for sem, val in wait_sems:
            sem.wait(s, val)
        for tc in range(TC):
            if tc >= 2:
                s_u.wait(s, tc - 1)
                s_red.wait(s, tc - 1)
                s_sq.wait(s, tc - 1)
            if gate_x:
                for sem, val in gate_x(tc):
                    sem.wait(s, val)
            s_xl[tc % 2].inc(s.dma_start(
                out=xsl(tc), in_=xsrc.ap()[tc * 128:(tc + 1) * 128, :]), 16)
            if tc >= 1:
                s_u.wait(s, tc)
                usl = usg.ap()[:, ((tc - 1) % 2) * D:((tc - 1) % 2 + 1) * D]
                s_w[(tc - 1) % 2].inc(s.dma_start(
                    out=uimg.ap()[(tc - 1) * 128:tc * 128, :], in_=usl), 16)
            if tc >= 4 and tc % 4 == 0:
                _emit_trans(s, tc // 4 - 1)
        s_u.wait(s, TC)
        usl = usg.ap()[:, ((TC - 1) % 2) * D:((TC - 1) % 2 + 1) * D]
        s_w[(TC - 1) % 2].inc(s.dma_start(
            out=uimg.ap()[(TC - 1) * 128:TC * 128, :], in_=usl), 16)
        _emit_trans(s, TC // 4 - 1)

    @block.vector
    def _(v):
        for sem, val in wait_sems:
            sem.wait(v, val)
        for it in range(TC + 2):
            tc2 = it - 2
            if 0 <= tc2 < TC:
                s_rstd.wait(v, tc2 + 1)
                s_nb.inc(v.tensor_scalar(stat(3, tc2), stat(0, tc2), stat(2, tc2),
                                         -1.0, OP.mult, OP.mult))
            tc1 = it - 1
            if 0 <= tc1 < TC:
                s_sq.wait(v, tc1 + 1)
                s_red.wait(v, tc1 + 1)
                v.tensor_scalar_mul(stat(0, tc1), stat(0, tc1), 1.0 / D)
                v.drain()
                v.tensor_tensor(stat(2, tc1), stat(0, tc1), stat(0, tc1), OP.mult)
                v.drain()
                s_var.inc(v.tensor_scalar(stat(1, tc1), stat(1, tc1), 1.0 / D,
                                          stat(2, tc1), OP.mult, OP.subtract))
            if it < TC:
                s_xl[it % 2].wait(v, 16 * (it // 2 + 1))
                s_red.inc(v.tensor_reduce(stat(0, it), xsl(it), AX.X, OP.add))

    @block.scalar
    def _(s):
        for sem, val in wait_sems:
            sem.wait(s, val)
        for it in range(TC + 2):
            tc2 = it - 2
            if 0 <= tc2 < TC:
                s_nb.wait(s, tc2 + 1)
                if tc2 >= 2:
                    s_w[tc2 % 2].wait(s, 16 * (tc2 // 2))
                dst = usg.ap()[:, (tc2 % 2) * D:(tc2 % 2 + 1) * D]
                s_u.inc(s.activation(dst, xsl(tc2), AF.Identity, bias=stat(3, tc2),
                                     scale=stat(2, tc2)))
            tc1 = it - 1
            if 0 <= tc1 < TC:
                s_var.wait(s, tc1 + 1)
                s.activation(stat(2, tc1), stat(1, tc1), AF.Ln, bias=ctx.cc(1e-5))
                s.drain()
                s_rstd.inc(s.activation(stat(2, tc1), stat(2, tc1), AF.Exp,
                                        scale=ctx.cc(-0.5)))
            if it < TC:
                s_xl[it % 2].wait(s, 16 * (it // 2 + 1))
                if it >= 1:
                    s_sq.wait(s, it)
                s_sq.inc(s.activation(sq.ap()[:, :], xsl(it), AF.Square,
                                      accum_out=stat(1, it)))

    finals = [(s, s.n) for s in (s_w[0], s_w[1], s_u, s_red, s_sq, s_var,
                                 s_rstd, s_nb, s_xl[0], s_xl[1], s_tt, s_tw)]
    return {"w": s_w, "tw": s_tw, "finals": finals}


def p_allgather(ctx, block, tag, in_names, out_names, wait_sems):
    cc = ctx.sem(tag + "_cc", g=True)

    @block.gpsimd
    def _(g):
        for sem, val in wait_sems:
            sem.wait(g, val)
        for inn, outn in zip(in_names, out_names):
            cc.inc(g.collective_compute(
                "AllGather", OP.bypass,
                replica_groups=[list(range(NCORES))],
                ins=[ctx.t[inn].ap().opt()],
                outs=[ctx.t[outn].ap().opt()],
            ))
    return cc


def p_ag_u(ctx, block, tag, s_tw, wait_sems):
    cc = ctx.sem(tag + "_cc", g=True)
    uTimg = ctx.t["uT_img"]
    guT = ctx.t["guT"]

    @block.gpsimd
    def _(g):
        for sem, val in wait_sems:
            sem.wait(g, val)
        for q in range(4):
            s_tw.wait(g, 16 * (2 * q + 2))
            cc.inc(g.collective_compute(
                "AllGather", OP.bypass,
                replica_groups=[list(range(NCORES))],
                ins=[uTimg.ap()[2 * q * D:(2 * q + 2) * D, :].opt()],
                outs=[guT.ap()[2 * q * NCORES * D:
                               (2 * q + 2) * NCORES * D, :].opt()],
            ))
    return cc


def p_qkv(ctx, block, kind, wait_sems, scope, cc_u):
    ctx.phase()
    pb = ctx.t["PB"].ap()
    gu = ctx.t["guT"]
    kimg, qimg, vimg = ctx.t["kimg"], ctx.t["qimg"], ctx.t["vimgF"]
    tg = kind + "qkv"
    isa = kind == "a"
    W = ctx.t
    pfx = kind

    wq = ctx.sbuf(tg + "_wq", [128, 8 * 128], BF16, scope)
    wk = ctx.sbuf(tg + "_wk", [128, 8 * 128], BF16, scope)
    wv = ctx.sbuf(tg + "_wv", [128, 8 * 128], BF16, scope)
    wf = ctx.sbuf(tg + "_wf", [128, 8 * 2], BF16, scope) if isa else None
    hm = ctx.sbuf(tg + "_hm", [128, 2], BF16, scope)
    bqs = ctx.sbuf(tg + "_bq", [128, 4], F32, scope)
    uT = ctx.sbuf(tg + "_uT", [128, 2 * 8 * 512], BF16, scope)
    qst = ctx.sbuf(tg + "_qst", [128, 2 * 512], BF16, scope)
    kst = ctx.sbuf(tg + "_kst", [128, 2 * 512], BF16, scope)
    vst = ctx.sbuf(tg + "_vst", [128, 2 * 512], BF16, scope)
    prod = ctx.sbuf(tg + "_prod", [128, 2 * 512], BF16, scope)
    dstg = ctx.sbuf(tg + "_dstg", [128, 2 * 512], F32, scope)
    d32 = ctx.sbuf(tg + "_d32", [32, SEQ], BF16, scope)
    off32 = ctx.sbuf(tg + "_off32", [32, SEQ], F32, scope)
    D32 = ctx.sbuf(tg + "_D32", [32, SEQ], F32, scope)
    qa = off32
    axbuf = ctx.sbuf(tg + "_axbuf", [32, SEQ], BF16, scope)
    ones32 = ctx.sbuf(tg + "_ones32", [32, SEQ], BF16, scope)

    s_wld = ctx.sem(tg + "_wld", g=True)
    s_w2 = ctx.sem(tg + "_w2")
    s_ut = ctx.sem(tg + "_ut")
    s_mmq = ctx.sem(tg + "_mmq")
    s_mmk = ctx.sem(tg + "_mmk")
    s_mmv = ctx.sem(tg + "_mmv")
    s_mmf = ctx.sem(tg + "_mmf")
    s_moff = ctx.sem(tg + "_moff")
    s_evq = ctx.sem(tg + "_evq")
    s_evk = ctx.sem(tg + "_evk")
    s_evv = ctx.sem(tg + "_evv")
    s_evo = ctx.sem(tg + "_evo")
    s_pr = ctx.sem(tg + "_pr")
    s_dst = ctx.sem(tg + "_dst")
    s_d32 = ctx.sem(tg + "_d32", g=True)
    s_wr = ctx.sem(tg + "_wr")
    s_aux = ctx.sem(tg + "_aux")
    s_auxw = ctx.sem(tg + "_auxw", g=True)

    ndma_tt = 4 if isa else 2

    @block.gpsimd
    def _(g):
        for sem, val in wait_sems:
            sem.wait(g, val)
        n0 = 0
        for nm, dst in (("w_q", wq), ("w_k", wk), ("w_v", wv)):
            src = W[pfx + nm].ap().rearrange("(c p) o -> p c o", p=128)
            s_wld.inc(g.dma_start(out=dst.ap().rearrange("p (c o) -> p c o", o=128),
                                  in_=src), 16)
        if isa:
            src = W["aw_f"].ap().rearrange("(c p) o -> p c o", p=128)
            s_wld.inc(g.dma_start(out=wf.ap().rearrange("p (c o) -> p c o", o=2),
                                  in_=src), 16)
        s_wld.inc(g.dma_start(out=hm.ap()[:, :], in_=W["headmask"].ap()[:, :]), 16)
        if kind == "b":
            for i, nm in enumerate(("bb_q", "bb_k", "bb_v")):
                s_wld.inc(g.dma_start(out=bqs.ap()[:, i:i + 1],
                                      in_=W[nm].ap().rearrange("(p o) -> p o", o=1)), 16)
    wld_dma = s_wld.n

    s_wld2 = ctx.sem(tg + "_wld2")

    @block.vector
    def _(v):
        s_wld.wait(v, wld_dma)
        if kind == "b":
            v.tensor_scalar_mul(bqs.ap()[:, 0:1], bqs.ap()[:, 0:1], 0.125)
            v.drain()
        if NTP < 32:
            v.memset(d32.ap()[:, :], 0.0)
            v.memset(off32.ap()[:, :], 0.0)
            v.memset(qst.ap()[:, :], 0.0)
            v.memset(kst.ap()[:, :], 0.0)
            v.memset(vst.ap()[:, :], 0.0)
            v.drain()
        s_wld2.inc(v.memset(ones32.ap()[:, :], 1.0))

    def _emit_img_writes(s, tp):
        for tt2 in range(2):
            tt = 2 * tp + tt2
            rank, b0 = tt // 8, 2 * (tt % 8)
            scol = rank * POS
            s_evq.wait(s, 2 * tp + tt2 + 1)
            s_evk.wait(s, 2 * tp + tt2 + 1)
            s_evv.wait(s, 2 * tp + tt2 + 1)
            for img, stg in ((qimg, qst), (kimg, kst)):
                for h in range(HL):
                    src = stg.ap()[64 * h:64 * h + 64, tt2 * 512:(tt2 + 1) * 512] \
                        .rearrange("f (b p) -> f b p", b=2)
                    dst = img.ap()[h:h + 1, b0:b0 + 2, 0:64, scol:scol + POS] \
                        .rearrange("o b f p -> (o f) b p")
                    s_wr.inc(s.dma_start(out=dst, in_=src), 16)
            src = vst.ap()[:, tt2 * 512:(tt2 + 1) * 512]
            s_wr.inc(s.dma_start(out=vimg.ap()[:, tt * 512:(tt + 1) * 512],
                                 in_=src), 16)

    def _ut_src(tt, c):
        sec, tb = tt // 8, tt % 8
        base = (tb // 2) * 2 * NCORES * D + sec * 2 * D + (tb % 2) * D + c * 128
        return gu.ap()[base:base + 128, :]

    @block.sync
    def _(s):
        for sem, val in wait_sems:
            sem.wait(s, val)
        for tp in range(NTP):
            if tp < 4:
                cc_u.wait(s, tp + 1)
            if tp >= 1:
                s_mmv.wait(s, 2 * tp)
                if isa:
                    s_mmf.wait(s, 2 * tp)
            for tt2 in range(2):
                tt = 2 * tp + tt2
                for c in range(4):
                    dst = uT.ap()[:, (tt2 * 8 + c) * 512:(tt2 * 8 + c + 1) * 512]
                    s_ut.inc(s.dma_start(out=dst, in_=_ut_src(tt, c)), 16)

    @block.tensor
    def _(t):
        s_wld.wait(t, wld_dma)
        s_wld2.wait(t, 1)
        for tp in range(NTP):
            s_ut.wait(t, 16 * 16 * (tp + 1))
            for wsb, bank, sem, evs in ((wq, 0, s_mmq, s_evq), (wk, 2, s_mmk, s_evk),
                                        (wv, 4, s_mmv, s_evv)):
                if tp >= 1:
                    evs.wait(t, 2 * tp)
                for tt2 in range(2):
                    for c in range(8):
                        ins = t.matmul(pb[:, (bank + tt2) * 512:(bank + tt2 + 1) * 512],
                                       wsb.ap()[:, c * 128:(c + 1) * 128],
                                       uT.ap()[:, (tt2 * 8 + c) * 512:(tt2 * 8 + c + 1) * 512],
                                       start=(c == 0), stop=(c == 7))
                    sem.inc(ins)
            if isa:
                if tp >= 1:
                    s_dst.wait(t, 2 * tp)
                for tt2 in range(2):
                    for c in range(8):
                        ins = t.matmul(pb[0:2, (6 + tt2) * 512:(6 + tt2 + 1) * 512],
                                       wf.ap()[:, c * 2:(c + 1) * 2],
                                       uT.ap()[:, (tt2 * 8 + c) * 512:(tt2 * 8 + c + 1) * 512],
                                       start=(c == 0), stop=(c == 7))
                    s_mmf.inc(ins)
            for tt2 in range(2):
                s_pr.wait(t, 2 * tp + tt2 + 1)
                if tp >= 1:
                    s_evo.wait(t, 2 * tp)
                s_moff.inc(t.matmul(pb[32:34, (6 + tt2) * 512:(6 + tt2 + 1) * 512],
                                    hm.ap()[:, :], prod.ap()[:, tt2 * 512:(tt2 + 1) * 512],
                                    start=True, stop=True))

    @block.scalar
    def _(s):
        for sem, val in wait_sems:
            sem.wait(s, val)
        s_wld.wait(s, wld_dma)
        s_wld2.wait(s, 1)
        for tp in range(NTP):
            if tp < 4:
                cc_u.wait(s, tp + 1)
            if tp >= 1:
                s_mmv.wait(s, 2 * tp)
                if isa:
                    s_mmf.wait(s, 2 * tp)
            for tt2 in range(2):
                tt = 2 * tp + tt2
                for c in range(4, 8):
                    dst = uT.ap()[:, (tt2 * 8 + c) * 512:(tt2 * 8 + c + 1) * 512]
                    s_ut.inc(s.dma_start(out=dst, in_=_ut_src(tt, c)), 16)
            if tp >= 1:
                _emit_img_writes(s, tp - 1)
            for tt2 in range(2):
                s_mmq.wait(s, 2 * tp + tt2 + 1)
                if tp >= 1:
                    s_wr.wait(s, 16 * 10 * tp)
                qbias = bqs.ap()[:, 0:1] if kind == "b" else ctx.cc(0.0)
                s_evq.inc(s.activation(qst.ap()[:, tt2 * 512:(tt2 + 1) * 512],
                                       pb[:, tt2 * 512:(tt2 + 1) * 512],
                                       AF.Identity, bias=qbias, scale=ctx.cc(0.125)))
            if isa:
                for tt2 in range(2):
                    s_mmf.wait(s, 2 * tp + tt2 + 1)
                    if tp >= 1:
                        s_d32.wait(s, 16 * ndma_tt * 2 * tp)
                    ds = dstg.ap()[0:2, tt2 * 512:(tt2 + 1) * 512]
                    s.activation(ds, pb[0:2, (6 + tt2) * 512:(6 + tt2 + 1) * 512],
                                 AF.Exp, scale=ctx.cc(-1.0, 2))
                    s.drain()
                    s_dst.inc(s.activation(ds, ds, AF.Ln, bias=ctx.cc(1.0, 2)))
        _emit_img_writes(s, NTP - 1)

    @block.vector
    def _(v):
        s_wld2.wait(v, 1)
        for tp in range(NTP):
            for tt2 in range(2):
                ksl = kst.ap()[:, tt2 * 512:(tt2 + 1) * 512]
                vsl = vst.ap()[:, tt2 * 512:(tt2 + 1) * 512]
                s_mmk.wait(v, 2 * tp + tt2 + 1)
                if tp >= 1:
                    s_wr.wait(v, 16 * 10 * tp)
                if kind == "b":
                    ins = v.tensor_single_scalar(ksl, pb[:, (2 + tt2) * 512:(3 + tt2) * 512],
                                                 bqs.ap()[:, 1:2], OP.add)
                else:
                    ins = v.tensor_copy(ksl, pb[:, (2 + tt2) * 512:(3 + tt2) * 512])
                s_evk.inc(ins)
                s_mmv.wait(v, 2 * tp + tt2 + 1)
                if kind == "b":
                    ins = v.tensor_single_scalar(vsl, pb[:, (4 + tt2) * 512:(5 + tt2) * 512],
                                                 bqs.ap()[:, 2:3], OP.add)
                else:
                    ins = v.tensor_copy(vsl, pb[:, (4 + tt2) * 512:(5 + tt2) * 512])
                s_evv.inc(ins)
                s_evq.wait(v, 2 * tp + tt2 + 1)
                v.drain()
                s_pr.inc(v.tensor_tensor(prod.ap()[:, tt2 * 512:(tt2 + 1) * 512],
                                         qst.ap()[:, tt2 * 512:(tt2 + 1) * 512], ksl,
                                         OP.mult))
                s_moff.wait(v, 2 * tp + tt2 + 1)
                if tp >= 1:
                    s_d32.wait(v, 16 * ndma_tt * 2 * tp)
                s_evo.inc(v.tensor_copy(dstg.ap()[32:34, tt2 * 512:(tt2 + 1) * 512],
                                        pb[32:34, (6 + tt2) * 512:(6 + tt2 + 1) * 512]))

    @block.gpsimd
    def _(g):
        for tp in range(NTP):
            for tt2 in range(2):
                tt = 2 * tp + tt2
                rank, b0 = tt // 8, 2 * (tt % 8)
                scol = rank * POS
                if isa:
                    s_dst.wait(g, 2 * tp + tt2 + 1)
                    for h in range(HL):
                        src = dstg.ap()[h:h + 1, tt2 * 512:(tt2 + 1) * 512] \
                            .rearrange("o (b p) -> o b p", b=2)
                        dst = d32.ap()[16 * h + b0:16 * h + b0 + 2, scol:scol + POS] \
                            .rearrange("b (o p) -> b o p", o=1)
                        s_d32.inc(g.dma_start(out=dst, in_=src), 16)
                s_evo.wait(g, 2 * tp + tt2 + 1)
                for h in range(HL):
                    src = dstg.ap()[32 + h:33 + h, tt2 * 512:(tt2 + 1) * 512] \
                        .rearrange("o (b p) -> o b p", b=2)
                    dst = off32.ap()[16 * h + b0:16 * h + b0 + 2, scol:scol + POS] \
                        .rearrange("b (o p) -> b o p", o=1)
                    s_d32.inc(g.dma_start(out=dst, in_=src), 16)

    tmpv = dstg

    def _split(v, val_ap, j, nsp):
        ins = v.tensor_copy(axbuf.ap()[:, :], val_ap)
        v.drain()
        if j < nsp - 1:
            for half in range(2):
                hs = half * 1024
                v.tensor_copy(tmpv.ap()[0:32, :], axbuf.ap()[:, hs:hs + 1024])
                v.drain()
                ins = v.tensor_tensor(val_ap[:, hs:hs + 1024], val_ap[:, hs:hs + 1024],
                                      tmpv.ap()[0:32, :], OP.subtract)
                v.drain()
        return ins

    @block.vector
    def _(v):
        s_d32.wait(v, s_d32.n)
        if isa:
            v.tensor_tensor_scan(D32.ap()[:, :], ones32.ap()[:, :], d32.ap()[:, :],
                                 0.0, OP.mult, OP.add)
            v.drain()
            v.tensor_tensor(qa.ap()[:, :], D32.ap()[:, :], off32.ap()[:, :], OP.add)
            v.drain()
        v.tensor_scalar_mul(qa.ap()[:, :], qa.ap()[:, :], -1.0)
        v.drain()
        for j in range(3):
            if j > 0:
                s_auxw.wait(v, 32 * j)
            s_aux.inc(_split(v, qa.ap()[:, :], j, 3))
        for j in range(3):
            s_auxw.wait(v, 32 * (3 + j))
            if isa:
                s_aux.inc(_split(v, D32.ap()[:, :], j, 3))
            else:
                s_aux.inc(v.memset(axbuf.ap()[:, :], 0.0))

    @block.gpsimd
    def _(g):
        for j in range(3):
            s_aux.wait(g, j + 1)
            for h in range(HL):
                src = axbuf.ap()[16 * h:16 * h + 16, :].rearrange("b (o s) -> b o s", o=1)
                dq = qimg.ap()[h:h + 1, :, 64 + j:65 + j, :].rearrange("o b r s -> b (o r) s")
                s_auxw.inc(g.dma_start(out=dq, in_=src), 16)
        for j in range(3):
            s_aux.wait(g, 3 + j + 1)
            for h in range(HL):
                src = axbuf.ap()[16 * h:16 * h + 16, :].rearrange("b (o s) -> b o s", o=1)
                dk = kimg.ap()[h:h + 1, :, 67 + j:68 + j, :].rearrange("o b r s -> b (o r) s")
                s_auxw.inc(g.dma_start(out=dk, in_=src), 16)
        s_wld2.wait(g, 1)
        for j in range(3):
            for h in range(HL):
                src = ones32.ap()[16 * h:16 * h + 16, :].rearrange("b (o s) -> b o s", o=1)
                dq = qimg.ap()[h:h + 1, :, 67 + j:68 + j, :].rearrange("o b r s -> b (o r) s")
                dk = kimg.ap()[h:h + 1, :, 64 + j:65 + j, :].rearrange("o b r s -> b (o r) s")
                s_auxw.inc(g.dma_start(out=dq, in_=src), 16)
                s_auxw.inc(g.dma_start(out=dk, in_=src), 16)

    finals = [(s_wr, s_wr.n), (s_auxw, s_auxw.n), (s_d32, s_d32.n),
              (s_aux, s_aux.n), (s_moff, s_moff.n), (s_evq, s_evq.n),
              (s_evk, s_evk.n), (s_evv, s_evv.n), (s_pr, s_pr.n),
              (s_evo, s_evo.n), (s_ut, s_ut.n), (s_wld2, 1)]
    if isa:
        finals += [(s_dst, s_dst.n), (s_mmf, s_mmf.n)]
    return {"imgs": s_wr, "auxw": s_auxw, "finals": finals}


ATT_SCHED = []
for _qcg in range(4):
    for _kb in range(4 * _qcg + 4):
        ATT_SCHED.append((_qcg, _kb, _kb - 4 * _qcg))
NSC = len(ATT_SCHED)


def p_attn(ctx, block, kind, wait_sems, scope):
    ctx.phase()
    pb = ctx.t["PB"].ap()
    kimg, qimg, vimg = ctx.t["kimg"], ctx.t["qimg"], ctx.t["vimgF"]
    yimg, rimg = ctx.t["yimgL"], ctx.t["rimg"]
    tg = kind + "att"

    masks = ctx.sbuf(tg + "_msk", [128, 4 * 512], F32, scope)
    kc = ctx.sbuf(tg + "_kc", [KR, 2 * HL * SEQ], BF16, scope)
    qc = ctx.sbuf(tg + "_qc", [KR, 2 * HL * SEQ], BF16, scope)
    vc = ctx.sbuf(tg + "_vc", [128, 2 * 16 * 130], BF16, scope)
    vstg = ctx.sbuf(tg + "_vstg", [128, 2 * 128], BF16, scope)
    probs = ctx.sbuf(tg + "_pb", [128, 8 * 512], BF16, scope)
    ybuf = ctx.sbuf(tg + "_yb", [64, 2 * 512], BF16, scope)
    rst = ctx.sbuf(tg + "_rst", [128, 2 * 512], F32, scope)

    s_mld = ctx.sem(tg + "_mld", g=True)

    @block.gpsimd
    def _(g):
        for sem, val in wait_sems:
            sem.wait(g, val)
        s_mld.inc(g.dma_start(out=masks.ap()[:, :], in_=ctx.t["masks"].ap()[:, :]), 16)

    s_kq = [ctx.sem(tg + "_kq0"), ctx.sem(tg + "_kq1")]
    s_vt = [ctx.sem(tg + "_vt0"), ctx.sem(tg + "_vt1")]
    s_vf = ctx.sem(tg + "_vf")
    s_sc = ctx.sem(tg + "_sc")
    s_ex = ctx.sem(tg + "_ex")
    s_mk = ctx.sem(tg + "_mk")
    s_pvs = ctx.sem(tg + "_pvs")
    s_ey = ctx.sem(tg + "_ey")
    s_yw = [ctx.sem(tg + "_yw0"), ctx.sem(tg + "_yw1")]
    s_rw = [ctx.sem(tg + "_rw0"), ctx.sem(tg + "_rw1")]

    def vcol(buf, kb, c):
        return buf * 16 * 130 + kb * 130 + c

    kq_done = []
    vt_done = []

    def _emit_writes(s, b):
        for h in range(HL):
            for qcg in range(4):
                grp = (b * HL + h) * 4 + qcg
                s_ey.wait(s, grp + 1)
                src = ybuf.ap()[:, (grp % 2) * 512:(grp % 2 + 1) * 512] \
                    .rearrange("f (r p) -> f r p", r=2)
                dst = yimg.ap()[64 * h:64 * h + 64, :] \
                    .rearrange("f (r bb p) -> f r bb p", r=NCORES, bb=NB) \
                    [:, 2 * qcg:2 * qcg + 2, b:b + 1, :] \
                    .rearrange("f r bb p -> f (r bb) p")
                s_yw[grp % 2].inc(s.dma_start(out=dst, in_=src), 16)
                rsrc = rst.ap()[64:65, (grp % 2) * 512:(grp % 2 + 1) * 512] \
                    .rearrange("o (r p) -> o r p", r=2)
                rdst = rimg.ap()[h:h + 1, :].rearrange("o (r bb p) -> o r bb p",
                                                       r=NCORES, bb=NB) \
                    [:, 2 * qcg:2 * qcg + 2, b:b + 1, :].rearrange("o r bb p -> o r (bb p)")
                s_rw[grp % 2].inc(s.dma_start(out=rdst, in_=rsrc), 16)

    @block.sync
    def _(s):
        for sem, val in wait_sems:
            sem.wait(s, val)
        for b in range(NB):
            if b >= 1:
                _emit_writes(s, b - 1)
            if b >= 2:
                s_pvs.wait(s, NSC * HL * (b - 1))
            buf = b % 2
            for h in range(HL):
                ksrc = kimg.ap()[h:h + 1, b:b + 1, :, :].rearrange("o bb r s -> (o bb r) s")
                s_kq[buf].inc(s.dma_start(
                    out=kc.ap()[:, (buf * HL + h) * SEQ:(buf * HL + h + 1) * SEQ],
                    in_=ksrc), 16)
                qsrc = qimg.ap()[h:h + 1, b:b + 1, :, :].rearrange("o bb r s -> (o bb r) s")
                s_kq[buf].inc(s.dma_start(
                    out=qc.ap()[:, (buf * HL + h) * SEQ:(buf * HL + h + 1) * SEQ],
                    in_=qsrc), 16)
            for kb in range(16):
                base = kb * 128
                gcol = (base // POS) * NT + b * POS + (base % POS)
                gkb = b * 16 + kb
                if gkb >= 2:
                    s_vf.wait(s, gkb - 1)
                dst = vstg.ap()[:, (gkb % 2) * 128:(gkb % 2 + 1) * 128]
                s_vt[gkb % 2].inc(s.dma_start(
                    out=dst, in_=vimg.ap()[0:128, gcol:gcol + 128],
                    transpose=True), 16)
            kq_done.append(s_kq[buf].n)
            vt_done.append((s_vt[0].n, s_vt[1].n))
        _emit_writes(s, NB - 1)

    s_vo = ctx.sem(tg + "_vo")

    @block.vector
    def _(v):
        for sem, val in wait_sems:
            sem.wait(v, val)
        ins = None
        for buf in range(2):
            for kb in range(16):
                for h in range(HL):
                    col = vcol(buf, kb, 65 * h) + 64
                    ins = v.memset(vc.ap()[:, col:col + 1], 1.0)
        s_vo.inc(ins)

    flat = [(b, h, qcg, kb) for b in range(NB) for h in range(HL)
            for (qcg, kb, rel) in ATT_SCHED]
    LAG = 4

    def _pv(t, gp):
        b, h, qcg, kb = flat[gp]
        grp = (b * HL + h) * 4 + qcg
        s_ex.wait(t, gp + 1)
        if kb == 0 and grp >= 2:
            s_ey.wait(t, grp - 1)
        ob = (4 + (qcg % 2)) * 512
        s_pvs.inc(t.matmul(
            pb[0:65, ob:ob + 512],
            vc.ap()[:, vcol(b % 2, kb, 65 * h):vcol(b % 2, kb, 65 * (h + 1))],
            probs.ap()[:, (gp % 8) * 512:(gp % 8 + 1) * 512],
            start=(kb == 0), stop=(kb == 4 * qcg + 3)))

    @block.tensor
    def _(t):
        s_vo.wait(t, 1)
        lastb = -1
        pvn = 0
        for g, (b, h, qcg, kb) in enumerate(flat):
            if b != lastb:
                while pvn < g:
                    _pv(t, pvn)
                    pvn += 1
                s_kq[b % 2].wait(t, kq_done[b])
                s_vf.wait(t, 16 * (b + 1))
                lastb = b
            if g >= 4:
                s_ex.wait(t, g - 3)
            kbase = ((b % 2) * HL + h) * SEQ
            s_sc.inc(t.matmul(
                pb[:, (g % 4) * 512:(g % 4 + 1) * 512],
                kc.ap()[:, kbase + kb * 128:kbase + kb * 128 + 128],
                qc.ap()[:, kbase + qcg * 512:kbase + (qcg + 1) * 512],
                start=True, stop=True))
            while pvn <= g - LAG:
                _pv(t, pvn)
                pvn += 1
        while pvn < len(flat):
            _pv(t, pvn)
            pvn += 1

    @block.scalar
    def _(s):
        g = 0
        for b in range(NB):
            for h in range(HL):
                nmask = 0
                for i, (qcg, kb, rel) in enumerate(ATT_SCHED):
                    s_sc.wait(s, g + 1)
                    if rel >= 0:
                        nmask = (g // NSC) * 16 + sum(1 for ii in range(g % NSC + 1)
                                                      if ATT_SCHED[ii][2] >= 0)
                        s_mk.wait(s, nmask)
                    if g >= 8:
                        s_pvs.wait(s, g - 7)
                    s_ex.inc(s.activation(probs.ap()[:, (g % 8) * 512:(g % 8 + 1) * 512],
                                          pb[:, (g % 4) * 512:(g % 4 + 1) * 512], AF.Exp))
                    g += 1

    @block.vector
    def _(v):
        for sem, val in wait_sems:
            sem.wait(v, val)
        s_mld.wait(v, 16)
        s_vo.wait(v, 1)
        g = 0
        grp = 0
        for b in range(NB):
            buf = b % 2
            for kb in range(16):
                gkb = b * 16 + kb
                s_vt[gkb % 2].wait(v, 16 * (gkb // 2 + 1))
                if b >= 2 and kb == 0:
                    s_pvs.wait(v, NSC * HL * (b - 1))
                sl = (gkb % 2) * 128
                v.tensor_copy(vc.ap()[:, vcol(buf, kb, 0):vcol(buf, kb, 0) + 64],
                              vstg.ap()[:, sl:sl + 64])
                s_vf.inc(v.tensor_copy(
                    vc.ap()[:, vcol(buf, kb, 65):vcol(buf, kb, 65) + 64],
                    vstg.ap()[:, sl + 64:sl + 128]))
            for h in range(HL):
                for i, (qcg, kb, rel) in enumerate(ATT_SCHED):
                    if rel >= 0:
                        s_sc.wait(v, g + 1)
                        ssl = pb[:, (g % 4) * 512:(g % 4 + 1) * 512]
                        s_mk.inc(v.tensor_tensor(
                            ssl, ssl, masks.ap()[:, rel * 512:(rel + 1) * 512], OP.add))
                    g += 1
                    if kb == 4 * qcg + 3:
                        s_pvs.wait(v, g)
                        if grp >= 2:
                            s_yw[grp % 2].wait(v, 16 * (grp // 2))
                            s_rw[grp % 2].wait(v, 16 * (grp // 2))
                        ob = (4 + (qcg % 2)) * 512
                        v.tensor_copy(ybuf.ap()[:, (grp % 2) * 512:(grp % 2 + 1) * 512],
                                      pb[0:64, ob:ob + 512])
                        v.drain()
                        s_ey.inc(v.tensor_copy(
                            rst.ap()[64:65, (grp % 2) * 512:(grp % 2 + 1) * 512],
                            pb[64:65, ob:ob + 512]))
                        grp += 1

    finals = [(s_yw[0], s_yw[0].n), (s_yw[1], s_yw[1].n),
              (s_rw[0], s_rw[0].n), (s_rw[1], s_rw[1].n),
              (s_ey, s_ey.n), (s_pvs, s_pvs.n), (s_sc, s_sc.n),
              (s_ex, s_ex.n), (s_mk, s_mk.n), (s_mld, 16), (s_vo, 1),
              (s_kq[0], s_kq[0].n), (s_kq[1], s_kq[1].n),
              (s_vt[0], s_vt[0].n), (s_vt[1], s_vt[1].n), (s_vf, s_vf.n)]
    return {"yw": s_yw, "rw": s_rw, "finals": finals}


def p_oproj(ctx, block, kind, wait_sems, scope, s_res, res_src):
    pb = ctx.t["PB"].ap()
    gy, gr = ctx.t["gy"], ctx.t["gr"]
    tg = kind + "opr"
    isb = kind == "b"
    W = ctx.t

    ymy = ctx.sbuf(tg + "_ymy", [128, 2 * 8 * 512], BF16, scope)
    rmy = ctx.sbuf(tg + "_rmy", [16, NT], F32, scope)
    rt = ctx.sbuf(tg + "_rt", [16, 2 * 512], F32, scope)
    rrb = ctx.sbuf(tg + "_rrb", [48, 2 * 512], BF16, scope)
    exps = ctx.sbuf(tg + "_exps", [48, 8 * 128], BF16, scope)
    wo = ctx.sbuf(tg + "_wo", [128, 8 * 1024], BF16, scope)
    bo2 = ctx.sbuf(tg + "_bo2", [2, 1024], BF16, scope)
    bot = ctx.sbuf(tg + "_bot", [1, 1024], F32, scope)
    on2 = ctx.sbuf(tg + "_on2", [2, 128], BF16, scope)
    yn = ctx.sbuf(tg + "_yn", [128, 8 * 512], BF16, scope)
    rkb = ctx.sbuf(tg + "_rk", [1, 1], U32, scope)

    s_ld = ctx.sem(tg + "_ld", g=True)
    s_ldy = [ctx.sem(tg + "_ldy0", g=True), ctx.sem(tg + "_ldy1", g=True)]
    s_ldr = ctx.sem(tg + "_ldr", g=True)
    s_rrb = ctx.sem(tg + "_rrb")
    s_nrm = ctx.sem(tg + "_nrm")
    s_rbc = ctx.sem(tg + "_rbc")
    s_yn = ctx.sem(tg + "_yn")
    s_omm = ctx.sem(tg + "_omm")

    s_rk = ctx.sem(tg + "_rk", g=True)
    s_bo = ctx.sem(tg + "_bo", g=True)
    blo = ctx.sbuf(tg + "_blo", [1, 1024], BF16, scope)

    @block.gpsimd
    def _(g):
        for sem, val in wait_sems:
            sem.wait(g, val)
        s_rk.inc(g.dma_start(out=rkb.ap()[:, :], in_=W["rank"].ap()[:, :]), 16)
        src = W[kind + "w_o"].ap().rearrange("(c p) o -> p c o", p=128)
        s_ld.inc(g.dma_start(out=wo.ap().rearrange("p (c o) -> p c o", o=1024),
                             in_=src), 16)
        s_ld.inc(g.dma_start(out=exps.ap().rearrange("p (c o) -> p c o", o=128),
                             in_=W["expander"].ap().rearrange("c p o -> p c o")), 16)
        if isb:
            s_ld.inc(g.dma_start(out=bot.ap()[:, :],
                                 in_=W["bb_o"].ap().rearrange("(o d) -> o d", o=1)), 16)
            s_nrm.wait(g, 1)
            s_bo.inc(g.dma_start(out=bo2.ap()[1:2, :], in_=blo.ap()[:, :]), 16)
        s_rk.wait(g, 16)
        with g.register(tg + "_r1") as r1, g.register(tg + "_r2") as r2:
            g.reg_load(r1, rkb.ap()[0:1, 0:1])
            g.reg_mul(r1, r1, NT)
            src = bass.AP(gr, r1, [[GT, 2 * NCORES], [1, NT]])
            s_ldr.inc(g.dma_start(out=rmy.ap(), in_=src), 16)
            for nt in range(8):
                if nt >= 2:
                    s_yn.wait(g, 8 * (nt - 1))
                g.reg_add(r2, r1, nt * 512)
                src = bass.AP(gy, r2, [[GT, 128], [128 * GT, NCORES], [1, 512]])
                dst = ymy.ap()[:, (nt % 2) * 8 * 512:((nt % 2) * 8 + 8) * 512] \
                    .rearrange("p (sec t) -> p sec t", t=512)
                s_ldy[nt % 2].inc(g.dma_start(out=dst, in_=src), 16)

    @block.vector
    def _(v):
        s_ld.wait(v, s_ld.n)
        v.memset(on2.ap()[:, :], 1.0)
        v.memset(rrb.ap()[:, :], 0.0)
        v.drain()
        if isb:
            v.tensor_copy(bo2.ap()[0:1, :], bot.ap()[:, :])
            v.drain()
            v.tensor_copy(rt.ap()[0:1, 0:1024], bo2.ap()[0:1, :])
            v.drain()
            v.tensor_tensor(bot.ap()[:, :], bot.ap()[:, :],
                            rt.ap()[0:1, 0:1024], OP.subtract)
            v.drain()
            v.tensor_copy(blo.ap()[:, :], bot.ap()[:, :])
        v.drain()
        s_nrm.inc(v.drain())

    @block.tensor
    def _(t):
        s_nrm.wait(t, 1)
        if isb:
            s_bo.wait(t, 16)
        for nt in range(8):
            s_rrb.wait(t, nt + 1)
            for c in range(8):
                s_yn.wait(t, max(0, 8 * nt + c + 1 - 2))
                s_rbc.inc(t.matmul(pb[:, (6 + (c % 2)) * 512:(7 + (c % 2)) * 512],
                                   exps.ap()[0:48, c * 128:(c + 1) * 128],
                                   rrb.ap()[0:48, (nt % 2) * 512:(nt % 2 + 1) * 512],
                                   start=True, stop=True))
            s_yn.wait(t, 8 * (nt + 1))
            for tk in range(4):
                if nt * 4 + tk >= 2:
                    s_res.wait(t, nt * 4 + tk - 1)
                pbank = ((nt * 4 + tk) % 2) * 2
                for oh in range(2):
                    for c in range(8):
                        ins = t.matmul(pb[:, (pbank + oh) * 512:(pbank + oh + 1) * 512],
                                       yn.ap()[:, c * 512 + tk * 128:c * 512 + tk * 128 + 128],
                                       wo.ap()[:, c * 1024 + oh * 512:c * 1024 + (oh + 1) * 512],
                                       start=(c == 0), stop=(c == 7 and not isb))
                    if isb:
                        ins = t.matmul(pb[:, (pbank + oh) * 512:(pbank + oh + 1) * 512],
                                       on2.ap()[:, :],
                                       bo2.ap()[:, oh * 512:(oh + 1) * 512],
                                       start=False, stop=True)
                    s_omm_last = ins
                s_omm.inc(s_omm_last)

    xr = ctx.sbuf(tg + "_xr", [128, 2 * D], F32, scope)
    xw = ctx.sbuf(tg + "_xw", [128, 2 * D], F32, scope)
    s_oxl = [ctx.sem(tg + "_oxl0"), ctx.sem(tg + "_oxl1")]
    s_oxw = [ctx.sem(tg + "_oxw0"), ctx.sem(tg + "_oxw1")]
    s_res2 = s_res
    xsrc = ctx.t[res_src]
    xres = ctx.t["xres"]

    @block.scalar
    def _(s):
        for sem, val in wait_sems:
            sem.wait(s, val)
        for tc in range(TC):
            if tc >= 2:
                s_res.wait(s, tc - 1)
            s_oxl[tc % 2].inc(s.dma_start(
                out=xr.ap()[:, (tc % 2) * D:(tc % 2 + 1) * D],
                in_=xsrc.ap()[tc * 128:(tc + 1) * 128, :]), 16)
            if tc >= 1:
                s_res.wait(s, tc)
                s_oxw[(tc - 1) % 2].inc(s.dma_start(
                    out=xres.ap()[(tc - 1) * 128:tc * 128, :],
                    in_=xw.ap()[:, ((tc - 1) % 2) * D:((tc - 1) % 2 + 1) * D]), 16)
        s_res.wait(s, TC)
        s_oxw[(TC - 1) % 2].inc(s.dma_start(
            out=xres.ap()[(TC - 1) * 128:TC * 128, :],
            in_=xw.ap()[:, ((TC - 1) % 2) * D:((TC - 1) % 2 + 1) * D]), 16)

    @block.vector
    def _(v):
        s_ld.wait(v, s_ld.n)
        s_nrm.wait(v, 1)
        for nt in range(8):
            if nt >= 2:
                s_rbc.wait(v, 8 * (nt - 1))
            sl = (nt % 2) * 512
            s_ldr.wait(v, 16)
            v.reciprocal(rt.ap()[:, sl:sl + 512], rmy.ap()[:, nt * 512:(nt + 1) * 512])
            v.drain()
            v.tensor_copy(rrb.ap()[0:16, sl:sl + 512], rt.ap()[:, sl:sl + 512])
            v.drain()
            so = (1 - nt % 2) * 512
            v.tensor_copy(rt.ap()[:, so:so + 512], rrb.ap()[0:16, sl:sl + 512])
            v.drain()
            v.tensor_tensor(rt.ap()[:, sl:sl + 512], rt.ap()[:, sl:sl + 512],
                            rt.ap()[:, so:so + 512], OP.subtract)
            v.drain()
            s_rrb.inc(v.tensor_copy(rrb.ap()[32:48, sl:sl + 512],
                                    rt.ap()[:, sl:sl + 512]))
            for c in range(8):
                s_rbc.wait(v, 8 * nt + c + 1)
                s_ldy[nt % 2].wait(v, 16 * (nt // 2 + 1))
                if nt >= 1:
                    s_omm.wait(v, 4 * nt)
                s_yn.inc(v.tensor_tensor(
                    yn.ap()[:, c * 512:(c + 1) * 512],
                    ymy.ap()[:, ((nt % 2) * 8 + c) * 512:
                             ((nt % 2) * 8 + c + 1) * 512],
                    pb[:, (6 + (c % 2)) * 512:(7 + (c % 2)) * 512],
                    OP.mult))
            for tk in range(4):
                s_omm.wait(v, 4 * nt + tk + 1)
                pbank = ((nt * 4 + tk) % 2) * 2
                tc = 4 * nt + tk
                s_oxl[tc % 2].wait(v, 16 * (tc // 2 + 1))
                if tc >= 2:
                    s_oxw[tc % 2].wait(v, 16 * (tc // 2))
                s_res.inc(v.tensor_tensor(
                    xw.ap()[:, (tc % 2) * D:(tc % 2 + 1) * D],
                    xr.ap()[:, (tc % 2) * D:(tc % 2 + 1) * D],
                    pb[:, pbank * 512:(pbank + 2) * 512], OP.add))

    finals = [(s_res, 32), (s_omm, s_omm.n), (s_yn, s_yn.n), (s_rrb, s_rrb.n),
              (s_rbc, s_rbc.n), (s_ld, s_ld.n), (s_ldr, s_ldr.n), (s_rk, 16),
              (s_ldy[0], s_ldy[0].n), (s_ldy[1], s_ldy[1].n), (s_nrm, 1),
              (s_oxl[0], s_oxl[0].n), (s_oxl[1], s_oxl[1].n),
              (s_oxw[0], s_oxw[0].n), (s_oxw[1], s_oxw[1].n)]
    if isb:
        finals.append((s_bo, 16))
    return finals, s_oxw



def p_ffn(ctx, block, tag, wait_sems, scope, res_src):
    ctx.phase()
    pb = ctx.t["PB"].ap()
    uimg = ctx.t["u_img"]
    uTimg = ctx.t["uT_img"]
    cimg = ctx.t["cimg"]
    xsrc = ctx.t[res_src]
    xres = ctx.t["xres"]
    W = ctx.t
    tg = tag

    wu = ctx.sbuf(tg + "_wu", [128, 8 * 3 * D], BF16, scope)
    ucw = ctx.sbuf(tg + "_ucw", [128, 8 * 3 * D], BF16, scope)
    wot = None
    uot = None
    cbuf = ctx.sbuf(tg + "_c", [128, 2 * D], F32, scope)
    cT = ctx.sbuf(tg + "_cT", [128, 8 * 256], BF16, scope)
    uT = ctx.sbuf(tg + "_uT", [128, 8 * 256], BF16, scope)
    gb = ctx.sbuf(tg + "_gb", [128, 3 * D], F32, scope)
    zb = ctx.sbuf(tg + "_zb", [128, 2 * D], F32, scope)
    hb = ctx.sbuf(tg + "_hb", [128, D], F32, scope)
    tmpc = hb
    u2 = ctx.sbuf(tg + "_u2", [128, 8 * 128], BF16, scope)
    xr = ctx.sbuf(tg + "_xr", [128, 2 * D], F32, scope)
    xw = ctx.sbuf(tg + "_xw", [128, 2 * D], F32, scope)
    blo = ctx.sbuf(tg + "_blo", [1, 3 * D], BF16, scope)
    prow = ctx.sbuf(tg + "_prow", [1, 3 * D], F32, scope)
    bgr = gb
    spl = gb
    bor = zb
    icr = zb
    baug = ctx.sbuf(tg + "_baug", [2, 3 * D], BF16, scope)
    zaug = ctx.sbuf(tg + "_zaug", [2, D], BF16, scope)
    iaug = ctx.sbuf(tg + "_iaug", [2, D], BF16, scope)
    on2 = ctx.sbuf(tg + "_on2", [2, 128], BF16, scope)

    s_wld = ctx.sem(tg + "_wld", g=True)
    s_w2 = ctx.sem(tg + "_w2", g=True)
    s_prep = ctx.sem(tg + "_prep")
    s_bsh = ctx.sem(tg + "_bsh", g=True)
    s_c0m = ctx.sem(tg + "_c0m")
    s_c0 = ctx.sem(tg + "_c0")
    s_ut = ctx.sem(tg + "_ut")
    s_ct = ctx.sem(tg + "_ct")
    s_cw = [ctx.sem(tg + "_cw0", g=True), ctx.sem(tg + "_cw1", g=True)]
    s_mm = ctx.sem(tg + "_mm")
    s_gev = ctx.sem(tg + "_gev")
    s_cup = ctx.sem(tg + "_cup")
    s_zct = ctx.sem(tg + "_zct")
    s_zmm = ctx.sem(tg + "_zmm")
    s_zev = ctx.sem(tg + "_zev")
    s_u2 = ctx.sem(tg + "_u2")
    s_hmm = ctx.sem(tg + "_hmm")
    s_hpre = ctx.sem(tg + "_hpre")
    s_tanh = ctx.sem(tg + "_tanh")
    s_xl = [ctx.sem(tg + "_xl0"), ctx.sem(tg + "_xl1")]
    s_res = ctx.sem(tg + "_res")
    s_xw = [ctx.sem(tg + "_xw0"), ctx.sem(tg + "_xw1")]

    NST = NB

    @block.gpsimd
    def _(g):
        for sem, val in wait_sems:
            sem.wait(g, val)
        for nm, dst, width in (("f_wu", wu, 3 * D), ("f_uc", ucw, 3 * D)):
            srcap = W[nm].ap().rearrange("(c p) j -> p c j", p=128)
            s_wld.inc(g.dma_start(out=dst.ap().rearrange("p (c j) -> p c j", j=width),
                                  in_=srcap), 16)
        s_wld.inc(g.dma_start(out=gb.ap()[0:1, 0:3 * D],
                              in_=W["f_bg"].ap().rearrange("(o j) -> o j", o=1)), 16)
        s_wld.inc(g.dma_start(out=zb.ap()[0:1, 0:D],
                              in_=W["f_bo"].ap().rearrange("(o j) -> o j", o=1)), 16)
        s_wld.inc(g.dma_start(out=zb.ap()[0:1, D:2 * D],
                              in_=W["f_inc"].ap().rearrange("(o j) -> o j", o=1)), 16)
        for i, (aug, wid) in enumerate(((baug, 3 * D), (zaug, D), (iaug, D))):
            s_prep.wait(g, i + 1)
            s_bsh.inc(g.dma_start(out=aug.ap()[1:2, :], in_=blo.ap()[:, 0:wid]), 16)
        for ch in range(2):
            s_c0.wait(g, ch + 1)
            s_cw[ch].inc(g.dma_start(out=cimg.ap()[ch * 128:(ch + 1) * 128, :],
                                     in_=cbuf.ap()[:, ch * D:(ch + 1) * D]), 16)
        for t in range(NST):
            for ch in range(2):
                s_cup.wait(g, 2 * t + ch + 1)
                s_cw[ch].inc(g.dma_start(out=cimg.ap()[ch * 128:(ch + 1) * 128, :],
                                         in_=cbuf.ap()[:, ch * D:(ch + 1) * D]), 16)
        s_mm.wait(g, 2 * NST)
        for nm, off in (("f_wot", 0), ("f_uot", 8 * D)):
            srcap = W[nm].ap().rearrange("(c p) j -> p c j", p=128)
            s_w2.inc(g.dma_start(
                out=wu.ap()[:, off:off + 8 * D].rearrange("p (c j) -> p c j", j=D),
                in_=srcap), 16)
    nwld = s_wld.n

    @block.vector
    def _(v):
        s_wld.wait(v, nwld)
        v.memset(on2.ap()[:, :], 1.0)
        rows = ((gb.ap()[0:1, 0:3 * D], baug, 3 * D),
                (zb.ap()[0:1, 0:D], zaug, D),
                (zb.ap()[0:1, D:2 * D], iaug, D))
        for ri, (srcr, aug, wid) in enumerate(rows):
            if ri >= 1:
                s_bsh.wait(v, 16 * ri)
            v.drain()
            v.tensor_copy(aug.ap()[0:1, :], srcr)
            v.drain()
            v.tensor_copy(prow.ap()[0:1, 0:wid], aug.ap()[0:1, :])
            v.drain()
            v.tensor_tensor(srcr, srcr, prow.ap()[0:1, 0:wid], OP.subtract)
            v.drain()
            s_prep.inc(v.tensor_copy(blo.ap()[:, 0:wid], srcr))
        s_c0m.wait(v, 1)
        for ch in range(2):
            s_c0.inc(v.tensor_copy(cbuf.ap()[:, ch * D:(ch + 1) * D],
                                   pb[:, 6 * 512:8 * 512]))
        for t in range(NST):
            for ch in range(2):
                s_gev.wait(v, 2 * t + ch + 1)
                s_cw[ch].wait(v, 16 * (t + 1))
                csl = cbuf.ap()[:, ch * D:(ch + 1) * D]
                v.tensor_tensor(tmpc.ap()[:, :], csl, gb.ap()[:, 0:D], OP.mult)
                v.tensor_tensor(gb.ap()[:, D:2 * D], gb.ap()[:, D:2 * D],
                                gb.ap()[:, 2 * D:3 * D], OP.mult)
                v.drain()
                s_cup.inc(v.tensor_tensor(csl, tmpc.ap()[:, :], gb.ap()[:, D:2 * D],
                                          OP.add))
        for ch in range(2):
            s_zmm.wait(v, ch + 1)
            s_zev.inc(v.tensor_copy(zb.ap()[:, ch * D:(ch + 1) * D],
                                    pb[:, (2 * ch) * 512:(2 * ch + 2) * 512]))
        v.drain()
        for tc in range(TC):
            s_hmm.wait(v, tc + 1)
            if tc >= 1:
                s_tanh.wait(v, tc)
                v.drain()
            hbank = (4 + 2 * (tc % 2)) * 512
            s_hpre.inc(v.scalar_tensor_tensor(
                hb.ap()[:, :], pb[:, hbank:hbank + 1024], 1.0,
                zb.ap()[:, (tc % 2) * D:(tc % 2 + 1) * D], OP.mult, OP.add))
            s_tanh.wait(v, tc + 1)
            s_xl[tc % 2].wait(v, 16 * (tc // 2 + 1))
            if tc >= 2:
                s_xw[tc % 2].wait(v, 16 * (tc // 2))
            s_res.inc(v.tensor_tensor(
                xw.ap()[:, (tc % 2) * D:(tc % 2 + 1) * D],
                xr.ap()[:, (tc % 2) * D:(tc % 2 + 1) * D], hb.ap()[:, :], OP.add))

    @block.scalar
    def _(s):
        for t in range(NST):
            for ch in range(2):
                s_mm.wait(s, 2 * t + ch + 1)
                if t > 0 or ch > 0:
                    s_cup.wait(s, 2 * t + ch)
                s.activation(gb.ap()[:, 0:D], pb[:, 0:2 * 512], AF.Sigmoid)
                s.activation(gb.ap()[:, D:2 * D], pb[:, 2 * 512:4 * 512], AF.Sigmoid)
                s_gev.inc(s.activation(gb.ap()[:, 2 * D:3 * D], pb[:, 4 * 512:6 * 512],
                                       AF.Tanh))
        for tc in range(TC):
            if tc >= 1:
                s_hmm.wait(s, tc)
            s_u2.inc(s.dma_start(
                out=u2.ap().rearrange("p (c n) -> p c n", n=128),
                in_=uTimg.ap()[(tc // 4) * D:(tc // 4 + 1) * D,
                               (tc % 4) * 128:(tc % 4 + 1) * 128]
                .rearrange("(c p) n -> p c n", p=128)), 16)
            if tc >= 2:
                s_res.wait(s, tc - 1)
            s_xl[tc % 2].inc(s.dma_start(
                out=xr.ap()[:, (tc % 2) * D:(tc % 2 + 1) * D],
                in_=xsrc.ap()[tc * 128:(tc + 1) * 128, :]), 16)
            s_hpre.wait(s, tc + 1)
            s_tanh.inc(s.activation(hb.ap()[:, :], hb.ap()[:, :], AF.Tanh))
            if tc >= 1:
                s_res.wait(s, tc)
                s_xw[(tc - 1) % 2].inc(s.dma_start(
                    out=xres.ap()[(tc - 1) * 128:tc * 128, :],
                    in_=xw.ap()[:, ((tc - 1) % 2) * D:((tc - 1) % 2 + 1) * D]), 16)
        s_res.wait(s, TC)
        s_xw[(TC - 1) % 2].inc(s.dma_start(
            out=xres.ap()[(TC - 1) * 128:TC * 128, :],
            in_=xw.ap()[:, ((TC - 1) % 2) * D:((TC - 1) % 2 + 1) * D]), 16)

    @block.sync
    def _(s):
        for sem, val in wait_sems:
            sem.wait(s, val)
        for t in range(NST):
            if t >= 1:
                s_mm.wait(s, 2 * t)
            s_cw[0].wait(s, 16 * (t + 1))
            s_cw[1].wait(s, 16 * (t + 1))
            for c8 in range(8):
                s_ct.inc(s.dma_start(
                    out=cT.ap()[:, c8 * 256:(c8 + 1) * 256],
                    in_=cimg.ap()[0:POS, c8 * 128:(c8 + 1) * 128],
                    transpose=True), 16)
            s_ut.inc(s.dma_start(
                out=uT.ap().rearrange("p (c n) -> p c n", n=256),
                in_=uTimg.ap()[(t // 2) * D:(t // 2 + 1) * D,
                               (t % 2) * 256:(t % 2 + 1) * 256]
                .rearrange("(c p) n -> p c n", p=128)), 16)

        s_mm.wait(s, 2 * NST)
        s_cw[0].wait(s, 16 * (NST + 1))
        s_cw[1].wait(s, 16 * (NST + 1))
        for c8 in range(8):
            s_zct.inc(s.dma_start(
                out=cT.ap()[:, c8 * 256:(c8 + 1) * 256],
                in_=cimg.ap()[0:POS, c8 * 128:(c8 + 1) * 128],
                transpose=True), 16)


    @block.tensor
    def _(t):
        s_wld.wait(t, nwld)
        s_bsh.wait(t, s_bsh.n)
        for half in range(2):
            ins = t.matmul(pb[:, (6 + half) * 512:(7 + half) * 512],
                           on2.ap()[:, :], iaug.ap()[:, half * 512:(half + 1) * 512],
                           start=True, stop=True)
        s_c0m.inc(ins)
        for ts in range(NST):
            s_ut.wait(t, 16 * (ts + 1))
            s_ct.wait(t, 16 * 8 * (ts + 1))
            for ch in range(2):
                if ts > 0 or ch > 0:
                    s_gev.wait(t, 2 * ts + ch)
                for n in range(6):
                    psl = pb[:, n * 512:(n + 1) * 512]
                    for c8 in range(8):
                        t.matmul(psl,
                                 uT.ap()[:, c8 * 256 + ch * 128:c8 * 256 + ch * 128 + 128],
                                 wu.ap()[:, c8 * 3 * D + n * 512:c8 * 3 * D + (n + 1) * 512],
                                 start=(c8 == 0), stop=False)
                    for c8 in range(8):
                        t.matmul(psl,
                                 cT.ap()[:, c8 * 256 + ch * 128:c8 * 256 + ch * 128 + 128],
                                 ucw.ap()[:, c8 * 3 * D + n * 512:c8 * 3 * D + (n + 1) * 512],
                                 start=False, stop=False)
                    ins = t.matmul(psl, on2.ap()[:, :],
                                   baug.ap()[:, n * 512:(n + 1) * 512],
                                   start=False, stop=True)
                s_mm.inc(ins)
        s_zct.wait(t, 16 * 8)
        s_w2.wait(t, 32)
        s_gev.wait(t, 2 * NST)
        for ch in range(2):
            for half in range(2):
                psl = pb[:, (2 * ch + half) * 512:(2 * ch + half + 1) * 512]
                for c8 in range(8):
                    t.matmul(psl,
                             cT.ap()[:, c8 * 256 + ch * 128:c8 * 256 + ch * 128 + 128],
                             wu.ap()[:, 8 * D + c8 * D + half * 512:
                                     8 * D + c8 * D + (half + 1) * 512],
                             start=(c8 == 0), stop=False)
                ins = t.matmul(psl, on2.ap()[:, :],
                               zaug.ap()[:, half * 512:(half + 1) * 512],
                               start=False, stop=True)
            s_zmm.inc(ins)
        for tc in range(TC):
            s_u2.wait(t, 16 * (tc + 1))
            if tc >= 2:
                s_hpre.wait(t, tc - 1)
            for half in range(2):
                hbank = (4 + 2 * (tc % 2) + half) * 512
                for c8 in range(8):
                    ins = t.matmul(pb[:, hbank:hbank + 512],
                                   u2.ap()[:, c8 * 128:(c8 + 1) * 128],
                                   wu.ap()[:, c8 * D + half * 512:
                                           c8 * D + (half + 1) * 512],
                                   start=(c8 == 0), stop=(c8 == 7))
            s_hmm.inc(ins)

    finals = [(s_res, TC), (s_hmm, TC), (s_tanh, TC), (s_hpre, TC),
              (s_mm, s_mm.n), (s_gev, s_gev.n), (s_cup, s_cup.n),
              (s_zmm, 2), (s_zev, 2), (s_cw[0], s_cw[0].n), (s_cw[1], s_cw[1].n),
              (s_ct, s_ct.n), (s_zct, s_zct.n), (s_ut, s_ut.n), (s_u2, s_u2.n),
              (s_xl[0], s_xl[0].n), (s_xl[1], s_xl[1].n),
              (s_xw[0], s_xw[0].n), (s_xw[1], s_xw[1].n),
              (s_wld, nwld), (s_bsh, s_bsh.n), (s_prep, 3), (s_c0, 2), (s_c0m, 1),
              (s_w2, 32)]
    return finals, s_xw, s_u2


def emit_attn_layer(ctx, block, ES, kind, prev, res_src, stop=None,
                    gate_x=None, gate_t=None):
    r = p_layernorm(ctx, block, kind + "ln", res_src, prev, gate_x, gate_t)
    fw = [(r["tw"], r["tw"].n)] + r["finals"]
    if stop == "ln":
        return {"fw": fw, "early": fw, "oxw": None}
    cc_u = p_ag_u(ctx, block, kind + "agu", r["tw"], prev)
    fw = [(cc_u, 1)] + r["finals"]
    if stop == "agu":
        return {"fw": fw, "early": fw, "oxw": None}
    with ES() as sc1:
        rq = p_qkv(ctx, block, kind, fw, sc1, cc_u)
    fw = rq["finals"] + [(cc_u, 4)] + r["finals"]
    if stop == "qkv":
        return {"fw": fw, "early": fw, "oxw": None}
    with ES() as sc2:
        ra = p_attn(ctx, block, kind, fw, sc2)
        fw = ra["finals"] + r["finals"]
        if stop == "attn":
            return {"fw": fw, "early": fw, "oxw": None}
        cc_y = p_allgather(ctx, block, kind + "agy", ["yimgL", "rimg"],
                           ["gy", "gr"],
                           [(ra["yw"][0], ra["yw"][0].n),
                            (ra["yw"][1], ra["yw"][1].n),
                            (ra["rw"][0], ra["rw"][0].n),
                            (ra["rw"][1], ra["rw"][1].n)])
        fw = [(cc_y, 2)] + fw
        early = fw
        if stop == "agy":
            return {"fw": fw, "early": fw, "oxw": None}
    with ES() as sc3:
        ctx.phase()
        s_res = ctx.sem(kind + "res")
        fw, oxw = p_oproj(ctx, block, kind, fw, sc3, s_res, res_src)
    return {"fw": fw, "early": early, "oxw": oxw}


def emit_ffn_layer(ctx, block, ES, tag, prev, res_src, stop=None,
                   gate_x=None, gate_t=None):
    r = p_layernorm(ctx, block, tag + "ln", res_src, prev, gate_x, gate_t)
    fw = r["finals"]
    if stop == "ln":
        return {"fw": fw, "early": fw, "xw": None, "u2": None}
    with ES() as sc:
        fw, xw, u2s = p_ffn(ctx, block, tag, fw, sc, res_src)
    return {"fw": fw, "early": r["finals"], "xw": xw, "u2": u2s}


def build_program(taps=(), upto="full"):
    nc = bass.Bass("TRN2", target_bir_lowering=False)
    from contextlib import ExitStack
    stack = ExitStack()
    ctx = Ctx(nc, stack)

    ctx.dram("x_shard", [NT, D], F32, kind="ExternalInput")
    ctx.dram("rank", [1, 1], U32, kind="ExternalInput")
    ctx.dram("headmask", [128, 2], BF16, kind="ExternalInput")
    ctx.dram("masks", [128, 4 * 512], F32, kind="ExternalInput")
    ctx.dram("expander", [8, 48, 128], BF16, kind="ExternalInput")
    for k in ("a", "b"):
        ctx.dram(k + "w_q", [D, 128], F32, kind="ExternalInput")
        ctx.dram(k + "w_k", [D, 128], F32, kind="ExternalInput")
        ctx.dram(k + "w_v", [D, 128], F32, kind="ExternalInput")
        ctx.dram(k + "w_o", [D, D], F32, kind="ExternalInput")
    ctx.dram("aw_f", [D, 2], F32, kind="ExternalInput")
    for nm in ("bb_q", "bb_k", "bb_v"):
        ctx.dram(nm, [128], F32, kind="ExternalInput")
    ctx.dram("bb_o", [D], F32, kind="ExternalInput")
    ctx.dram("f_wu", [D, 3 * D], F32, kind="ExternalInput")
    ctx.dram("f_uc", [D, 3 * D], F32, kind="ExternalInput")
    ctx.dram("f_wot", [D, D], F32, kind="ExternalInput")
    ctx.dram("f_uot", [D, D], F32, kind="ExternalInput")
    ctx.dram("f_bg", [3 * D], F32, kind="ExternalInput")
    ctx.dram("f_bo", [D], F32, kind="ExternalInput")
    ctx.dram("f_inc", [D], F32, kind="ExternalInput")

    ctx.dram("xres", [NT, D], F32)
    ctx.dram("u_img", [NT, D], BF16)
    ctx.dram("uT_img", [8 * D, 512], BF16)
    ctx.dram("guT", [8 * NCORES * D, 512], BF16)
    ctx.dram("kimg", [HL, NB, KR, SEQ], BF16)
    ctx.dram("qimg", [HL, NB, KR, SEQ], BF16)
    ctx.dram("vimgF", [128, GT], BF16)
    ctx.dram("yimgL", [128, GT], BF16)
    ctx.dram("rimg", [HL, GT], F32)
    ctx.dram("gy", [NCORES * 128, GT], BF16)
    ctx.dram("gr", [NCORES * HL, GT], F32)
    ctx.dram("cimg", [POS, D], BF16)

    ctx.dram("x_out", [NT, D], F32, kind="ExternalOutput")

    taps = set(taps)
    tap_handles = {}

    def mktap(name, shape, dtype):
        if name in taps:
            tap_handles[name] = ctx.dram("tap_" + name, shape, dtype,
                                         kind="ExternalOutput")

    mktap("u_img", [NT, D], BF16)
    mktap("kimg", [HL, NB, KR, SEQ], BF16)
    mktap("qimg", [HL, NB, KR, SEQ], BF16)
    mktap("vimgF", [128, GT], BF16)
    mktap("yimgL", [128, GT], BF16)
    mktap("rimg", [HL, GT], F32)
    mktap("xres", [NT, D], F32)

    with stack:
        ctx.sbuf("constpool", [128, len(CONST_VALUES)], F32)
        ctx.sbuf("xbuf", [128, 2 * D], F32)
        ctx.sbuf("ln_stats", [128, TC * 4], F32)
        ctx.sbuf("ln_ustage", [128, 2 * D], BF16)
        ctx.sbuf("ln_sqscratch", [128, D], BF16)
        ctx.sbuf("ln_uts", [128, 8 * 512], BF16)
        ctx.psum("PB", [128, 4096], F32)

        from contextlib import ExitStack as ES
        with nc.Block() as block:
            cs = ctx.emit_consts(block)
            fw = [(cs, 1)]

            def oxw_gate(oxw):
                return lambda tc: [(oxw[tc % 2], 16 * (tc // 2 + 1))]

            stop = None
            if upto.startswith("a:"):
                stop = upto.split(":")[1]
            ra = emit_attn_layer(ctx, block, ES, "a", fw, "x_shard", stop)
            fw = ra["fw"]
            if stop is None and upto not in ("a",):
                stop2 = upto.split(":")[1] if upto.startswith("f1:") else None
                rf1 = emit_ffn_layer(ctx, block, ES, "f1", ra["early"], "xres",
                                     stop2, gate_x=oxw_gate(ra["oxw"]))
                fw = rf1["fw"]
                if stop2 is None and upto not in ("f1",):
                    stop3 = upto.split(":")[1] if upto.startswith("b:") else None
                    rb = emit_attn_layer(
                        ctx, block, ES, "b", rf1["early"], "xres", stop3,
                        gate_x=oxw_gate(rf1["xw"]),
                        gate_t=lambda g: [(rf1["u2"], 16 * (4 * g + 4))])
                    fw = rb["fw"]
                    if stop3 is None and upto not in ("b",):
                        stop4 = upto.split(":")[1] if upto.startswith("f2:") else None
                        rf2 = emit_ffn_layer(ctx, block, ES, "f2", rb["early"],
                                             "xres", stop4,
                                             gate_x=oxw_gate(rb["oxw"]))
                        fw = rf2["fw"]

            ctx.phase()
            fin = ctx.sem("fin", g=True)

            @block.gpsimd
            def _(g):
                for sem, val in fw:
                    sem.wait(g, val)
                for name, th in tap_handles.items():
                    fin.inc(g.dma_start(out=th.ap().opt(), in_=ctx.t[name].ap().opt()), 16)
                if upto in ("full", "a", "f1", "b", "f2"):
                    fin.inc(g.dma_start(out=ctx.t["x_out"].ap().opt(),
                                        in_=ctx.t["xres"].ap().opt()), 16)
                fin.wait(g, fin.n)

    return nc



def make_consts():
    m = np.zeros((128, 4, 4, 128), np.float32)
    for rel in range(4):
        for qq in range(4):
            if qq < rel:
                m[:, rel, qq, :] = -1e30
            elif qq == rel:
                m[:, rel, qq, :] = np.where(
                    np.triu(np.ones((128, 128), bool)), 0.0, -1e30)
    masks = m.reshape(128, 4 * 512).astype(np.float32)
    hmv = np.zeros((128, 2), np.float32)
    hmv[0:64, 0] = 1.0
    hmv[64:128, 1] = 1.0
    headmask = hmv.astype(BF)
    ex = np.zeros((8, 48, 128), np.float32)
    for c in range(8):
        for j in range(128):
            hgl = 2 * c + (j // 64)
            ex[c, hgl, j] = 1.0
            ex[c, 32 + hgl, j] = 1.0
    expander = ex.astype(BF)
    return masks, headmask, expander


def shard_inputs(inputs):
    x = np.ascontiguousarray(inputs["L0_attn_a_x"])
    masks, headmask, expander = make_consts()
    f32c = np.ascontiguousarray
    maps = []
    for r in range(NCORES):
        xs = f32c(x[:, r * POS:(r + 1) * POS, :].reshape(NT, D)).astype(np.float32)
        cm = {
            "x_shard": xs,
            "rank": np.array([[r]], np.uint32),
            "headmask": headmask, "masks": masks, "expander": expander,
            "aw_q": f32c(inputs["L0_attn_a_Wq"][:, 128 * r:128 * (r + 1)]),
            "aw_k": f32c(inputs["L0_attn_a_Wk"][:, 128 * r:128 * (r + 1)]),
            "aw_v": f32c(inputs["L0_attn_a_Wv"][:, 128 * r:128 * (r + 1)]),
            "aw_o": f32c(inputs["L0_attn_a_Wo"]),
            "aw_f": f32c(inputs["L0_attn_a_Wf"][:, 2 * r:2 * (r + 1)]),
            "bw_q": f32c(inputs["L0_attn_b_W_q"][:, 128 * r:128 * (r + 1)]),
            "bw_k": f32c(inputs["L0_attn_b_W_k"][:, 128 * r:128 * (r + 1)]),
            "bw_v": f32c(inputs["L0_attn_b_W_v"][:, 128 * r:128 * (r + 1)]),
            "bw_o": f32c(inputs["L0_attn_b_W_o"]),
            "bb_q": f32c(inputs["L0_attn_b_b_q"][128 * r:128 * (r + 1)]),
            "bb_k": f32c(inputs["L0_attn_b_b_k"][128 * r:128 * (r + 1)]),
            "bb_v": f32c(inputs["L0_attn_b_b_v"][128 * r:128 * (r + 1)]),
            "bb_o": f32c(inputs["L0_attn_b_b_o"]),
            "f_wu": f32c(np.concatenate([np.asarray(inputs["L0_ffn_wf"]).T,
                                         np.asarray(inputs["L0_ffn_wi"]).T,
                                         np.asarray(inputs["L0_ffn_wc"]).T], axis=1)),
            "f_uc": f32c(np.concatenate([np.asarray(inputs["L0_ffn_uf"]).T,
                                         np.asarray(inputs["L0_ffn_ui"]).T,
                                         np.asarray(inputs["L0_ffn_uc"]).T], axis=1)),
            "f_wot": f32c(np.asarray(inputs["L0_ffn_wo"]).T),
            "f_uot": f32c(np.asarray(inputs["L0_ffn_uo"]).T),
            "f_bg": f32c(np.concatenate([inputs["L0_ffn_bf"], inputs["L0_ffn_bi"],
                                         inputs["L0_ffn_bc"]])),
            "f_bo": f32c(inputs["L0_ffn_bo"]),
            "f_inc": f32c(inputs["L0_ffn_in_c"]),
        }
        maps.append(cm)
    return maps


def assemble(results):
    full = np.empty((NB, SEQ, D), np.float32)
    for r, res in enumerate(results):
        full[:, r * POS:(r + 1) * POS, :] = res["x_out"].reshape(NB, POS, D)
    return full


def kernel(**inputs):
    nc = build_program()
    maps = shard_inputs(inputs)
    res = run_bass_kernel_spmd(nc, maps, core_ids=list(range(NCORES)))
    return assemble(res.results)


if __name__ == "__main__":
    pass
